# version 03 (final)
# baseline (speedup 1.0000x reference)
import contextlib
import numpy as np
import ml_dtypes
import concourse.bass as bass
import concourse.mybir as mybir
from concourse.bass_utils import run_bass_kernel_spmd

F32 = mybir.dt.float32
BF16 = mybir.dt.bfloat16
I32 = mybir.dt.int32
ALU = mybir.AluOpType
AF = mybir.ActivationFunctionType
AX = mybir.AxisListType

NCORES = 8
T = 1024
D = 4096
II = 2048
NT = 8
NKD = 32
NKI = 16
LN2 = float(np.log(2.0))
IL2 = 1.0 / LN2

ENGS = ("vector", "scalar", "gpsimd", "tensor", "sync")
ESEM = {"vector": "d", "scalar": "a", "gpsimd": "g", "tensor": "p", "sync": None}


class Sched:
    def __init__(self):
        self.ops = {e: [] for e in ENGS}
        self.cnt = {}
        self.B = {}

    def bw(self, n):
        return self.B.setdefault(n, [None, []])

    def ptok(self):
        return ("p", self.cnt.get("p", 0))

    def add(self, eng, build, deps=(), inc=True, dma=None):
        w = {}
        for t in deps:
            if t is None:
                continue
            s, v = t
            if v > w.get(s, 0):
                w[s] = v
        if dma is not None:
            self.cnt[dma] = self.cnt.get(dma, 0) + 16
            tok = (dma, self.cnt[dma])
            self.ops[eng].append((w, build, dma, 16))
            return tok
        if inc:
            sem = ESEM[eng]
            self.cnt[sem] = self.cnt.get(sem, 0) + 1
            tok = (sem, self.cnt[sem])
            self.ops[eng].append((w, build, sem, 1))
            return tok
        self.ops[eng].append((w, build, None, 0))
        return None

    def op(self, eng, build, r=(), wr=(), extra=(), dma=None, inc=True):
        deps = list(extra)
        for n in r:
            deps.append(self.bw(n)[0])
        for n in wr:
            e = self.bw(n)
            deps.append(e[0])
            deps.extend(e[1])
        tok = self.add(eng, build, deps=deps, inc=inc, dma=dma)
        if tok is not None:
            for n in r:
                self.bw(n)[1].append(tok)
            for n in wr:
                self.B[n] = [tok, []]
        return tok

    def flush(self, block, sems):
        ops = self.ops
        self.ops = {e: [] for e in ENGS}
        for en in ENGS:
            def mk(en):
                def emit(eng):
                    waited = {}
                    for w, build, sem, amt in ops[en]:
                        for s, v in w.items():
                            if v > waited.get(s, 0):
                                eng.wait_ge(sems[s], v)
                                waited[s] = v
                        instr = build(eng)
                        if sem is not None:
                            instr.then_inc(sems[sem], amt)
                return emit
            getattr(block, en)(mk(en))


def build_nc():
    nc = bass.Bass("TRN2", target_bir_lowering=False)
    x_in = nc.dram_tensor("x", [T, D], F32, kind="ExternalInput").ap()
    w_att = nc.dram_tensor("w_att", [D, D], F32, kind="ExternalInput").ap()
    m_att = nc.dram_tensor("m_att", [D, D], I32, kind="ExternalInput").ap()
    wg_in = nc.dram_tensor("w_gate", [II, D], F32, kind="ExternalInput").ap()
    wu_in = nc.dram_tensor("w_up", [II, D], F32, kind="ExternalInput").ap()
    wd_in = nc.dram_tensor("w_down", [D, II], F32, kind="ExternalInput").ap()
    invg_in = nc.dram_tensor("invg", [128, D], F32, kind="ExternalInput").ap()
    invu_in = nc.dram_tensor("invu", [128, D], F32, kind="ExternalInput").ap()
    invi_in = nc.dram_tensor("invi", [128, II], F32, kind="ExternalInput").ap()
    swg_in = nc.dram_tensor("swg", [128, II], F32, kind="ExternalInput").ap()
    swu_in = nc.dram_tensor("swu", [128, II], F32, kind="ExternalInput").ap()
    swd_in = nc.dram_tensor("swd", [128, D], F32, kind="ExternalInput").ap()
    idb_in = nc.dram_tensor("idb", [128, 128], BF16, kind="ExternalInput").ap()
    idf_in = nc.dram_tensor("idf", [128, 128], F32, kind="ExternalInput").ap()
    y_out = nc.dram_tensor("y", [T, D], F32, kind="ExternalOutput").ap()
    wgT_d = nc.dram_tensor("wgT_d", [D, II], BF16).ap()
    wuT_d = nc.dram_tensor("wuT_d", [D, II], BF16).ap()
    wdT_d = nc.dram_tensor("wdT_d", [II, D], BF16).ap()
    wmh_d = nc.dram_tensor("wmh_d", [D, D], BF16).ap()
    wml_d = nc.dram_tensor("wml_d", [D, D], BF16).ap()

    S = Sched()

    with contextlib.ExitStack() as _st:
        E = _st.enter_context
        sem_d = E(nc.semaphore("sd")); sem_a = E(nc.semaphore("sa"))
        sem_g = E(nc.semaphore("sg")); sem_p = E(nc.semaphore("sp"))
        dma_sems = {}
        for nm in ["qc0", "qc1", "qpin", "qpm", "qsg0", "qsg1", "qx", "qv",
                   "qw", "qd0", "qd1", "qm0", "qm1", "qxc0", "qxc1",
                   "qy0", "qy1"]:
            dma_sems[nm] = E(nc.semaphore(nm))
        idb = E(nc.sbuf_tensor("idb_s", [128, 128], BF16))
        idf = E(nc.sbuf_tensor("idf_s", [128, 128], F32))
        xbuf = E(nc.sbuf_tensor("xbuf", [128, D], F32))
        ws = E(nc.sbuf_tensor("ws", [128, 6 * II], F32))
        vb = E(nc.sbuf_tensor("vb", [128, D], BF16))
        qq = E(nc.sbuf_tensor("qq", [128, D], BF16))
        qiT = E(nc.sbuf_tensor("qiT", [128, NKI * T], BF16))
        sml = E(nc.sbuf_tensor("sml", [128, 64], F32))
        sibuf = E(nc.sbuf_tensor("sibuf", [128, 8], F32))
        sems = {"d": sem_d, "a": sem_a, "g": sem_g, "p": sem_p}
        sems.update(dma_sems)

        def wsl(i, w=II):
            return ws[:, i * II:i * II + w]

        S.op("sync", lambda e: e.dma_start(out=idb[:, :], in_=idb_in),
             wr=["idb"], dma="qc0")
        S.op("sync", lambda e: e.dma_start(out=idf[:, :], in_=idf_in),
             wr=["idf"], dma="qc0")
        S.op("vector", lambda e: e.tensor_reduce(
            out=sml[:, 40:41], in_=idf[:, :], axis=AX.X, op=ALU.add),
            r=["idf"], wr=["sml"])
        S.op("vector", lambda e: e.tensor_scalar(
            out=sml[:, 41:42], in0=sml[:, 40:41], scalar1=float(2.0 ** -30),
            scalar2=None, op0=ALU.add), r=["sml"], wr=["sml"])
        S.op("vector", lambda e: e.tensor_scalar(
            out=sml[:, 42:43], in0=sml[:, 41:42], scalar1=-1.0,
            scalar2=float(2.0 ** 30), op0=ALU.add, op1=ALU.mult),
            r=["sml"], wr=["sml"])

        def ln_tile():
            S.op("vector", lambda e: e.tensor_reduce(
                out=sml[:, 2:3], in_=xbuf[:, :], axis=AX.X, op=ALU.add),
                r=["xbuf"], wr=["sml"])
            S.op("vector", lambda e: e.tensor_scalar(
                out=sml[:, 3:4], in0=sml[:, 2:3], scalar1=-1.0 / 4096.0,
                scalar2=None, op0=ALU.mult), r=["sml"], wr=["sml"])
            for h in range(2):
                S.op("vector", lambda e, h=h: e.tensor_scalar(
                    out=wsl(h), in0=xbuf[:, h * II:(h + 1) * II],
                    scalar1=sml[:, 3:4], scalar2=None, op0=ALU.add),
                    r=["xbuf", "sml"], wr=["ws%d" % h])
                S.op("vector", lambda e, h=h: e.tensor_tensor(
                    out=wsl(2), in0=wsl(h), in1=wsl(h), op=ALU.mult),
                    r=["ws%d" % h], wr=["ws2"])
                S.op("vector", lambda e, h=h: e.tensor_reduce(
                    out=sml[:, 4 + h:5 + h], in_=wsl(2), axis=AX.X, op=ALU.add),
                    r=["ws2"], wr=["sml"])
            S.op("vector", lambda e: e.tensor_tensor(
                out=sml[:, 6:7], in0=sml[:, 4:5], in1=sml[:, 5:6], op=ALU.add),
                r=["sml"], wr=["sml"])
            S.op("vector", lambda e: e.tensor_scalar(
                out=sml[:, 7:8], in0=sml[:, 6:7], scalar1=1.0 / 4096.0,
                scalar2=1e-5, op0=ALU.mult, op1=ALU.add),
                r=["sml"], wr=["sml"])
            S.op("scalar", lambda e: e.activation(
                out=sml[:, 7:8], in_=sml[:, 7:8], func=AF.Sqrt),
                r=["sml"], wr=["sml"])
            S.op("vector", lambda e: e.reciprocal(
                out=sml[:, 0:1], in_=sml[:, 7:8]), r=["sml"], wr=["sml"])
            for h in range(2):
                S.op("scalar", lambda e, h=h: e.activation(
                    out=wsl(h), in_=wsl(h), func=AF.Copy,
                    scale=sml[:, 0:1]), r=["sml"], wr=["ws%d" % h])

        C29 = float(2.0 ** 29 + 1.0)
        C45 = float(2.0 ** 45 + 1.0)

        def bf16r(src, bsrc, dst, bdst, s2, s3, s4, s5, b2, b3, b4, b5):
            S.op("vector", lambda e: e.tensor_scalar(
                out=s2, in0=src, scalar1=C29, scalar2=None, op0=ALU.mult),
                r=[bsrc], wr=[b2])
            S.op("vector", lambda e: e.tensor_tensor(
                out=s3, in0=s2, in1=src, op=ALU.subtract),
                r=[b2, bsrc], wr=[b3])
            S.op("vector", lambda e: e.tensor_tensor(
                out=s2, in0=s2, in1=s3, op=ALU.subtract), r=[b2, b3], wr=[b2])
            S.op("vector", lambda e: e.tensor_scalar(
                out=s3, in0=s2, scalar1=C45, scalar2=None, op0=ALU.mult),
                r=[b2], wr=[b3])
            S.op("vector", lambda e: e.tensor_tensor(
                out=s4, in0=s3, in1=s2, op=ALU.subtract), r=[b2, b3], wr=[b4])
            S.op("vector", lambda e: e.tensor_tensor(
                out=s2, in0=s3, in1=s4, op=ALU.subtract), r=[b3, b4], wr=[b2])
            S.op("scalar", lambda e: e.activation(
                out=dst, in_=src, func=AF.Copy), r=[bsrc], wr=[bdst])
            S.op("vector", lambda e: e.tensor_tensor(
                out=s3, in0=s2, in1=dst, op=ALU.subtract),
                r=[b2, bdst], wr=[b3])
            S.op("vector", lambda e: e.tensor_scalar(
                out=s3, in0=s3, scalar1=sml[:, 42:43], scalar2=None,
                op0=ALU.mult), r=[b3, "sml"], wr=[b3])
            S.op("vector", lambda e: e.tensor_tensor(
                out=dst, in0=dst, in1=s3, op=ALU.add), r=[b3, bdst], wr=[bdst])

        def quant_full(vbsrc, bvbsrc, W, inv_src, scix, qdst, bqdst):
            nh = W // II
            for h in range(nh):
                S.op("scalar", lambda e, h=h: e.dma_start(
                    out=wsl(5), in_=inv_src[:, h * II:(h + 1) * II]),
                    wr=["ws5"], dma="qv")
                S.op("vector", lambda e, h=h: e.tensor_tensor(
                    out=wsl(h), in0=vbsrc[:, h * II:(h + 1) * II],
                    in1=wsl(5), op=ALU.mult),
                    r=[bvbsrc, "ws5"], wr=["ws%d" % h])
                S.op("vector", lambda e, h=h: e.tensor_reduce(
                    out=sml[:, scix + 2 + h:scix + 3 + h], in_=wsl(h),
                    axis=AX.X, op=ALU.max, apply_absolute_value=True),
                    r=["ws%d" % h], wr=["sml"])
            if nh == 2:
                S.op("vector", lambda e: e.tensor_tensor(
                    out=sml[:, scix:scix + 1], in0=sml[:, scix + 2:scix + 3],
                    in1=sml[:, scix + 3:scix + 4], op=ALU.max),
                    r=["sml"], wr=["sml"])
            else:
                S.op("vector", lambda e: e.tensor_copy(
                    out=sml[:, scix:scix + 1], in_=sml[:, scix + 2:scix + 3]),
                    r=["sml"], wr=["sml"])
            S.op("vector", lambda e: e.tensor_scalar(
                out=sml[:, scix:scix + 1], in0=sml[:, scix:scix + 1],
                scalar1=1.0 / 127.0, scalar2=1e-8, op0=ALU.mult, op1=ALU.max),
                r=["sml"], wr=["sml"])
            S.op("vector", lambda e: e.reciprocal(
                out=sml[:, scix + 1:scix + 2], in_=sml[:, scix:scix + 1]),
                r=["sml"], wr=["sml"])
            for h in range(nh):
                xs = wsl(h)
                bxs = "ws%d" % h
                S.op("scalar", lambda e, xs=xs: e.activation(
                    out=wsl(2), in_=xs, func=AF.Abs), r=[bxs], wr=["ws2"])
                S.op("scalar", lambda e, xs=xs: e.activation(
                    out=wsl(3), in_=xs, func=AF.Sign), r=[bxs], wr=["ws3"])
                S.op("scalar", lambda e: e.activation(
                    out=wsl(2), in_=wsl(2), func=AF.Copy,
                    scale=sml[:, scix + 1:scix + 2]),
                    r=["ws2", "sml"], wr=["ws2"])
                qsl = qdst[:, h * II:(h + 1) * II]
                S.op("vector", lambda e: e.tensor_scalar(
                    out=wsl(2), in0=wsl(2), scalar1=128.0, scalar2=None,
                    op0=ALU.add), r=["ws2"], wr=["ws2"])
                S.op("scalar", lambda e, qsl=qsl: e.activation(
                    out=qsl, in_=wsl(2), func=AF.Copy), r=["ws2"], wr=[bqdst])
                S.op("vector", lambda e: e.tensor_scalar(
                    out=wsl(4), in0=wsl(2), scalar1=C45, scalar2=None,
                    op0=ALU.mult), r=["ws2"], wr=["ws4"])
                S.op("vector", lambda e: e.tensor_tensor(
                    out=wsl(5), in0=wsl(4), in1=wsl(2), op=ALU.subtract),
                    r=["ws2", "ws4"], wr=["ws5"])
                S.op("vector", lambda e: e.tensor_tensor(
                    out=wsl(4), in0=wsl(4), in1=wsl(5), op=ALU.subtract),
                    r=["ws4", "ws5"], wr=["ws4"])
                S.op("vector", lambda e, qsl=qsl: e.tensor_tensor(
                    out=wsl(5), in0=wsl(4), in1=qsl, op=ALU.subtract),
                    r=["ws4", bqdst], wr=["ws5"])
                S.op("vector", lambda e: e.tensor_scalar(
                    out=wsl(5), in0=wsl(5), scalar1=sml[:, 42:43],
                    scalar2=None, op0=ALU.mult),
                    r=["ws5", "sml"], wr=["ws5"])
                S.op("vector", lambda e, qsl=qsl: e.tensor_tensor(
                    out=wsl(5), in0=qsl, in1=wsl(5), op=ALU.add),
                    r=["ws5", bqdst], wr=["ws5"])
                S.op("vector", lambda e: e.scalar_tensor_tensor(
                    out=wsl(2), in0=wsl(5), scalar=-128.0, in1=wsl(3),
                    op0=ALU.add, op1=ALU.mult), r=["ws5", "ws3"], wr=["ws2"])
                S.op("vector", lambda e, qsl=qsl: e.tensor_copy(
                    out=qsl, in_=wsl(2)), r=["ws2"], wr=[bqdst])

        with contextlib.ExitStack() as _s1:
            E1 = _s1.enter_context
            pin = E1(nc.sbuf_tensor("pin", [128, D], F32))
            pbh = E1(nc.sbuf_tensor("pbh", [128, D], BF16))
            pbl = E1(nc.sbuf_tensor("pbl", [128, D], BF16))
            pmask = E1(nc.sbuf_tensor("pmask", [128, 1024], I32))
            stg = E1(nc.sbuf_tensor("stg", [128, NKD * 128], BF16))
            wstr = E1(nc.sbuf_tensor("wstr", [128, NKD * 128], BF16))
            qT2 = E1(nc.sbuf_tensor("qT2", [128, 2 * NKD * 128], BF16))
            gu = E1(nc.sbuf_tensor("gu", [128, 2 * II], BF16))
            ps_m0 = E1(nc.psum_tensor([128, 512], F32))
            ps_m1 = E1(nc.psum_tensor([128, 512], F32))
            ps_t0 = E1(nc.psum_tensor([128, 128], BF16))
            ps_t1 = E1(nc.psum_tensor([128, 128], BF16))
            ps_f0 = E1(nc.psum_tensor([128, 128], F32))
            ps_f1 = E1(nc.psum_tensor([128, 128], F32))
            blk1 = E1(nc.Block())
            psT = (ps_t0, ps_t1)
            psF = (ps_f0, ps_f1)
            ps_m = (ps_m0, ps_m1)

            def transpose128(src, bsrc, c, dst, bdst, ident, bid, pidx):
                if ident is idf:
                    pt = psF[pidx % 2]
                    bp = "psf%d" % (pidx % 2)
                else:
                    pt = psT[pidx % 2]
                    bp = "pst%d" % (pidx % 2)
                S.op("tensor", lambda e, c=c, pt=pt: e.transpose(
                    pt[:, :], src[:, c * 128:(c + 1) * 128], ident[:, :]),
                    r=[bsrc, bid], wr=[bp])
                S.op("scalar", lambda e, dst=dst, pt=pt: e.activation(
                    out=dst, in_=pt[:, :], func=AF.Copy), r=[bp], wr=[bdst])

            def prep_ffn(win, wout, nrow_blk, ncol_blk):
                for r_ in range(nrow_blk):
                    S.op("sync", lambda e, r_=r_: e.dma_start(
                        out=pin[:, :ncol_blk * 128],
                        in_=win[r_ * 128:(r_ + 1) * 128, :]),
                        wr=["pin"], dma="qpin")
                    for c in range(ncol_blk):
                        transpose128(pin, "pin", c,
                                     stg[:, c * 128:(c + 1) * 128], "stg",
                                     idf, "idf", c)
                    for h in range(2):
                        hw = ncol_blk // 2
                        S.op("gpsimd", lambda e, r_=r_, h=h, hw=hw, wout=wout:
                             e.dma_start(
                                out=wout.rearrange("(c p) f -> p c f", p=128)[
                                    :, h * hw:(h + 1) * hw,
                                    r_ * 128:(r_ + 1) * 128],
                                in_=stg[:, h * hw * 128:(h + 1) * hw * 128]
                                .rearrange("p (c f) -> p c f", f=128)),
                            r=["stg"], dma="qsg%d" % h)

            prep_ffn(wg_in, wgT_d, NKI, NKD)
            prep_ffn(wu_in, wuT_d, NKI, NKD)
            prep_ffn(wd_in, wdT_d, NKD, NKI)

            for r_ in range(NKD):
                S.op("sync", lambda e, r_=r_: e.dma_start(
                    out=pin[:, :], in_=w_att[r_ * 128:(r_ + 1) * 128, :]),
                    wr=["pin"], dma="qpin")
                for h in range(4):
                    S.op("sync", lambda e, r_=r_, h=h: e.dma_start(
                        out=pmask[:, :],
                        in_=m_att[r_ * 128:(r_ + 1) * 128,
                                  h * 1024:(h + 1) * 1024]),
                        wr=["pmask"], dma="qpin")
                    S.op("gpsimd", lambda e, h=h: e.tensor_copy(
                        out=wsl(0, 1024), in_=pmask[:, :]),
                        r=["pmask"], wr=["ws0"])
                    S.op("vector", lambda e, h=h: e.tensor_tensor(
                        out=pin[:, h * 1024:(h + 1) * 1024],
                        in0=pin[:, h * 1024:(h + 1) * 1024], in1=wsl(0, 1024),
                        op=ALU.mult), r=["ws0", "pin"], wr=["pin"])
                S.op("scalar", lambda e: e.activation(
                    out=pbh[:, :], in_=pin[:, :], func=AF.Copy),
                    r=["pin"], wr=["pbh"])
                S.op("vector", lambda e: e.tensor_tensor(
                    out=pbl[:, :], in0=pin[:, :], in1=pbh[:, :],
                    op=ALU.subtract), r=["pin", "pbh"], wr=["pbl"])
                for (src, bsrc, wout) in ((pbh, "pbh", wmh_d),
                                          (pbl, "pbl", wml_d)):
                    for c in range(NKD):
                        transpose128(src, bsrc, c,
                                     stg[:, c * 128:(c + 1) * 128], "stg",
                                     idb, "idb", c)
                    for h in range(2):
                        S.op("gpsimd", lambda e, r_=r_, h=h,
                             wout=wout: e.dma_start(
                                out=wout.rearrange("(c p) f -> p c f", p=128)[
                                    :, h * 16:(h + 1) * 16,
                                    r_ * 128:(r_ + 1) * 128],
                                in_=stg[:, h * 16 * 128:(h + 1) * 16 * 128]
                                .rearrange("p (c f) -> p c f", f=128)),
                            r=["stg"], dma="qsg%d" % h)

            for tix in range(NT):
                S.op("scalar", lambda e, tix=tix: e.dma_start(
                    out=xbuf[:, :], in_=x_in[tix * 128:(tix + 1) * 128, :]),
                    wr=["xbuf"], dma="qx")
                ln_tile()
                for h in range(2):
                    bf16r(wsl(h), "ws%d" % h, vb[:, h * II:(h + 1) * II],
                          "vb", wsl(2), wsl(3), wsl(4), wsl(5),
                          "ws2", "ws3", "ws4", "ws5")
                for qi_, invsrc in ((0, invg_in), (1, invu_in)):
                    quant_full(vb, "vb", D, invsrc, 8 + 8 * qi_, qq, "qq")
                    for c in range(NKD):
                        transpose128(
                            qq, "qq", c,
                            qT2[:, (qi_ * NKD + c) * 128:
                                (qi_ * NKD + c + 1) * 128],
                            "qT2", idb, "idb", c)
                for mi, wsrc in ((0, wgT_d), (1, wuT_d)):
                    for ic in range(NKI):
                        wb = "wstr0"
                        woff = 0
                        for h in range(2):
                            S.op("sync", lambda e, wsrc=wsrc, ic=ic, h=h,
                                 woff=woff: e.dma_start(
                                    out=wstr[:, woff + h * 16 * 128:
                                             woff + (h + 1) * 16 * 128]
                                    .rearrange("p (c f) -> p c f", f=128),
                                    in_=wsrc.rearrange("(c p) f -> p c f",
                                                       p=128)[
                                        :, h * 16:(h + 1) * 16,
                                        ic * 128:(ic + 1) * 128]),
                                wr=[wb], dma="qw")
                        pm = ps_m[ic % 2]
                        bpm = "psm%d" % (ic % 2)
                        qbase = mi * NKD
                        tok = None
                        for k in range(NKD):
                            deps = [S.bw(wb)[0], S.bw("qT2")[0]]
                            if k == 0:
                                ee = S.bw(bpm)
                                deps.append(ee[0])
                                deps.extend(ee[1])
                            tok = S.add(
                                "tensor",
                                lambda e, k=k, pm=pm, woff=woff,
                                qbase=qbase: e.matmul(
                                    pm[:, :128],
                                    qT2[:, (qbase + k) * 128:
                                        (qbase + k + 1) * 128],
                                    wstr[:, woff + k * 128:
                                         woff + (k + 1) * 128],
                                    start=(k == 0), stop=(k == NKD - 1)),
                                deps=deps, inc=(k == NKD - 1))
                        S.bw(wb)[1].append(tok)
                        S.bw("qT2")[1].append(tok)
                        S.B[bpm] = [tok, []]
                        sw_src = swg_in if mi == 0 else swu_in
                        gslice = gu[:, mi * II + ic * 128:
                                    mi * II + (ic + 1) * 128]
                        S.op("scalar", lambda e, sw_src=sw_src, ic=ic:
                             e.dma_start(
                                out=wsl(5, 128),
                                in_=sw_src[:, ic * 128:(ic + 1) * 128]),
                            wr=["ws5"], dma="qv")
                        S.op("vector", lambda e, gslice=gslice, pm=pm:
                             e.tensor_tensor(
                                out=gslice, in0=pm[:, :128],
                                in1=wsl(5, 128), op=ALU.mult),
                            r=[bpm, "ws5"], wr=["gu"])
                gsl = gu[:, 0:II]
                usl = gu[:, II:2 * II]
                S.op("scalar", lambda e, gsl=gsl: e.activation(
                    out=wsl(0), in_=gsl, func=AF.Silu,
                    scale=sml[:, 8:9]), r=["gu", "sml"], wr=["ws0"])
                S.op("vector", lambda e, usl=usl: e.tensor_scalar(
                    out=wsl(1), in0=usl, scalar1=sml[:, 16:17],
                    scalar2=None, op0=ALU.mult),
                    r=["gu", "sml"], wr=["ws1"])
                S.op("vector", lambda e: e.tensor_tensor(
                    out=wsl(1), in0=wsl(1), in1=wsl(0), op=ALU.mult),
                    r=["ws0", "ws1"], wr=["ws1"])
                S.op("gpsimd", lambda e: e.tensor_scalar(
                    out=wsl(1), in0=wsl(1), scalar1=10.0, scalar2=-10.0,
                    op0=ALU.min, op1=ALU.max), r=["ws1"], wr=["ws1"])
                bf16r(wsl(1), "ws1", vb[:, 0:II], "vb",
                      wsl(2), wsl(3), wsl(4), wsl(5),
                      "ws2", "ws3", "ws4", "ws5")
                quant_full(vb[:, 0:II], "vb", II, invi_in, 24, qq[:, 0:II], "qq")
                S.op("vector", lambda e, tix=tix: e.tensor_copy(
                    out=sibuf[:, tix:tix + 1], in_=sml[:, 24:25]),
                    r=["sml"], wr=["sibuf"])
                for c in range(NKI):
                    transpose128(
                        qq, "qq", c,
                        qiT[:, (c * NT + tix) * 128:
                            (c * NT + tix + 1) * 128],
                        "qiT", idb, "idb", c)
            S.flush(blk1, sems)

        with contextlib.ExitStack() as _s2:
            E2 = _s2.enter_context
            xnT = E2(nc.sbuf_tensor("xnT", [128, 4 * 2 * NKD * 128], BF16))
            xh = E2(nc.sbuf_tensor("xh", [128, D], BF16))
            gbuf = E2(nc.sbuf_tensor("gbuf", [128, 4 * 512], F32))
            wmst = E2(nc.sbuf_tensor("wmst", [128, 4 * 512], BF16))
            wdst = E2(nc.sbuf_tensor("wdst", [128, 2 * 512], BF16))
            xck = E2(nc.sbuf_tensor("xck", [128, 2 * 512], F32))
            yck = E2(nc.sbuf_tensor("yck", [128, 2 * 512], F32))
            ps_q0 = E2(nc.psum_tensor([128, 512], F32))
            ps_q1 = E2(nc.psum_tensor([128, 512], F32))
            ps_q2 = E2(nc.psum_tensor([128, 512], F32))
            ps_q3 = E2(nc.psum_tensor([128, 512], F32))
            ps_s0 = E2(nc.psum_tensor([128, 128], BF16))
            ps_s1 = E2(nc.psum_tensor([128, 128], BF16))
            blk2 = E2(nc.Block())
            psT2 = (ps_s0, ps_s1)
            psQ = (ps_q0, ps_q1, ps_q2, ps_q3)
            for quad in range(2):
                for tt_ in range(4):
                    tix = quad * 4 + tt_
                    S.op("scalar", lambda e, tix=tix: e.dma_start(
                        out=xbuf[:, :], in_=x_in[tix * 128:(tix + 1) * 128, :]),
                        wr=["xbuf"], dma="qx")
                    ln_tile()
                    for h in range(2):
                        S.op("scalar", lambda e, h=h: e.activation(
                            out=xh[:, h * II:(h + 1) * II], in_=wsl(h),
                            func=AF.Copy), r=["ws%d" % h], wr=["xh"])
                        S.op("vector", lambda e, h=h: e.tensor_tensor(
                            out=vb[:, h * II:(h + 1) * II], in0=wsl(h),
                            in1=xh[:, h * II:(h + 1) * II], op=ALU.subtract),
                            r=["ws%d" % h, "xh"], wr=["vb"])
                    for c in range(NKD):
                        for half, (srct, bsrc) in enumerate(
                                ((xh, "xh"), (vb, "vb"))):
                            pt = psT2[c % 2]
                            bp = "ps2%d" % (c % 2)
                            S.op("tensor", lambda e, c=c, pt=pt, srct=srct:
                                 e.transpose(
                                    pt[:, :], srct[:, c * 128:(c + 1) * 128],
                                    idb[:, :]), r=[bsrc, "idb"], wr=[bp])
                            S.op("scalar", lambda e, c=c, pt=pt, tt_=tt_,
                                 half=half: e.activation(
                                    out=xnT[:, ((tt_ * 2 + half) * NKD + c)
                                            * 128:((tt_ * 2 + half) * NKD
                                                   + c + 1) * 128],
                                    in_=pt[:, :], func=AF.Copy),
                                r=[bp], wr=["xnT"])
                for n in range(8):
                    for ic in range(NKI):
                        wb = "wd%d" % (ic % 2)
                        deps_w = []
                        if ic >= 2:
                            deps_w.append(("p", S.cnt.get("p", 0)))
                        S.op("sync", lambda e, ic=ic, n=n: e.dma_start(
                            out=wdst[:, (ic % 2) * 512:(ic % 2 + 1) * 512],
                            in_=wdT_d[ic * 128:(ic + 1) * 128,
                                      n * 512:(n + 1) * 512]),
                            wr=[wb], extra=deps_w, dma="qd%d" % (ic % 2))
                        for tt_ in range(4):
                            tix = quad * 4 + tt_
                            pm = psQ[tt_]
                            bpm = "psq%d" % tt_
                            deps = [S.bw(wb)[0], S.bw("qiT")[0]]
                            if ic == 0:
                                ee = S.bw(bpm)
                                deps.append(ee[0])
                                deps.extend(ee[1])
                            S.add("tensor", lambda e, ic=ic, tix=tix,
                                  pm=pm: e.matmul(
                                      pm[:, :],
                                      qiT[:, (ic * NT + tix) * 128:
                                          (ic * NT + tix + 1) * 128],
                                      wdst[:, (ic % 2) * 512:
                                           (ic % 2 + 1) * 512],
                                      start=(ic == 0),
                                      stop=(ic == NKI - 1)),
                                  deps=deps, inc=(tt_ == 3))
                        if ic == NKI - 1:
                            for tt_ in range(4):
                                S.B["psq%d" % tt_] = [S.ptok(), []]
                        S.bw("qiT")[1].append(S.ptok())
                        S.bw(wb)[1].append(S.ptok())
                    S.op("scalar", lambda e, n=n: e.dma_start(
                        out=wsl(5, 512),
                        in_=swd_in[:, n * 512:(n + 1) * 512]),
                        wr=["ws5"], dma="qv")
                    for tt_ in range(4):
                        tix = quad * 4 + tt_
                        pm = psQ[tt_]
                        bpm = "psq%d" % tt_
                        S.op("vector", lambda e, pm=pm, n=n: e.tensor_tensor(
                            out=wsl(0, 512), in0=pm[:, :],
                            in1=wsl(5, 512), op=ALU.mult),
                            r=[bpm, "ws5"], wr=["ws0"])
                        S.op("scalar", lambda e, tt_=tt_, tix=tix: e.activation(
                            out=gbuf[:, tt_ * 512:(tt_ + 1) * 512],
                            in_=wsl(0, 512), func=AF.Silu,
                            scale=sibuf[:, tix:tix + 1]),
                            r=["ws0", "sibuf"], wr=["gbuf"])
                    for k in range(NKD):
                        wb = "wm%d" % (k % 2)
                        deps_w = []
                        if k >= 2:
                            deps_w.append(("p", S.cnt.get("p", 0)))
                        for half, wsrc in ((0, wmh_d), (1, wml_d)):
                            S.op("sync", lambda e, k=k, n=n, half=half,
                                 wsrc=wsrc: e.dma_start(
                                    out=wmst[:, ((k % 2) * 2 + half) * 512:
                                             ((k % 2) * 2 + half + 1) * 512],
                                    in_=wsrc[k * 128:(k + 1) * 128,
                                             n * 512:(n + 1) * 512]),
                                wr=[wb], extra=deps_w, dma="qm%d" % (k % 2))
                        for tt_ in range(4):
                            pm = psQ[tt_]
                            bpm = "psq%d" % tt_
                            hi = xnT[:, ((tt_ * 2) * NKD + k) * 128:
                                     ((tt_ * 2) * NKD + k + 1) * 128]
                            lo = xnT[:, ((tt_ * 2 + 1) * NKD + k) * 128:
                                     ((tt_ * 2 + 1) * NKD + k + 1) * 128]
                            wh = wmst[:, ((k % 2) * 2) * 512:
                                      ((k % 2) * 2 + 1) * 512]
                            wl = wmst[:, ((k % 2) * 2 + 1) * 512:
                                      ((k % 2) * 2 + 2) * 512]
                            deps = [S.bw(wb)[0], S.bw("xnT")[0]]
                            if k == 0:
                                ee = S.bw(bpm)
                                deps.append(ee[0])
                                deps.extend(ee[1])
                            S.add("tensor", lambda e, pm=pm, hi=hi, wh=wh, k=k:
                                  e.matmul(pm[:, :], hi, wh, start=(k == 0),
                                           stop=False), deps=deps, inc=False)
                            S.add("tensor", lambda e, pm=pm, hi=hi, wl=wl:
                                  e.matmul(pm[:, :], hi, wl, start=False,
                                           stop=False), deps=[], inc=False)
                            S.add("tensor", lambda e, pm=pm, lo=lo,
                                  wh=wh, k=k: e.matmul(
                                      pm[:, :], lo, wh, start=False,
                                      stop=(k == NKD - 1)),
                                  deps=[], inc=(tt_ == 3))
                        if k == NKD - 1:
                            for tt_ in range(4):
                                S.B["psq%d" % tt_] = [S.ptok(), []]
                        S.bw("xnT")[1].append(S.ptok())
                        S.bw(wb)[1].append(S.ptok())
                    for tt_ in range(4):
                        tix = quad * 4 + tt_
                        pm = psQ[tt_]
                        bpm = "psq%d" % tt_
                        xcs = "xck%d" % (tt_ % 2)
                        ycs = "yck%d" % (tt_ % 2)
                        S.op("scalar", lambda e, tix=tix, n=n, tt_=tt_:
                             e.dma_start(
                                out=xck[:, (tt_ % 2) * 512:
                                        (tt_ % 2 + 1) * 512],
                                in_=x_in[tix * 128:(tix + 1) * 128,
                                         n * 512:(n + 1) * 512]),
                            wr=[xcs], dma="qxc%d" % (tt_ % 2))
                        S.op("vector", lambda e, pm=pm, tt_=tt_:
                             e.tensor_tensor(
                                out=wsl(0, 512), in0=pm[:, :],
                                in1=gbuf[:, tt_ * 512:(tt_ + 1) * 512],
                                op=ALU.mult), r=[bpm, "gbuf"], wr=["ws0"])
                        S.op("vector", lambda e, tt_=tt_: e.tensor_tensor(
                            out=yck[:, (tt_ % 2) * 512:(tt_ % 2 + 1) * 512],
                            in0=wsl(0, 512),
                            in1=xck[:, (tt_ % 2) * 512:(tt_ % 2 + 1) * 512],
                            op=ALU.add), r=["ws0", xcs], wr=[ycs])
                        S.op("gpsimd", lambda e, tix=tix, n=n, tt_=tt_:
                             e.dma_start(
                                out=y_out[tix * 128:(tix + 1) * 128,
                                          n * 512:(n + 1) * 512],
                                in_=yck[:, (tt_ % 2) * 512:
                                        (tt_ % 2 + 1) * 512]),
                            r=[ycs], dma="qy%d" % (tt_ % 2))
            S.add("vector", lambda e: e.tensor_copy(
                out=sml[:, 60:61], in_=sml[:, 61:62]),
                deps=[("qy0", S.cnt["qy0"]), ("qy1", S.cnt["qy1"])])
            S.flush(blk2, sems)
    return nc


def _pack_inputs(inputs):
    xs = np.ascontiguousarray(
        np.asarray(inputs["L0_attn_data"], dtype=np.float32).reshape(8192, D))
    def rep(v, w):
        return np.ascontiguousarray(np.broadcast_to(
            np.asarray(v, dtype=np.float32)[None, :w], (128, w)))
    idb = np.eye(128, dtype=ml_dtypes.bfloat16)
    idf = np.eye(128, dtype=np.float32)
    base = {
        "w_att": np.ascontiguousarray(np.asarray(inputs["L0_attn_weight"],
                                                 dtype=np.float32)),
        "m_att": np.ascontiguousarray(np.asarray(inputs["L0_attn_mask"],
                                                 dtype=np.int32)),
        "w_gate": np.ascontiguousarray(np.asarray(inputs["L0_ffn_w_gate"],
                                                  dtype=np.float32)),
        "w_up": np.ascontiguousarray(np.asarray(inputs["L0_ffn_w_up"],
                                                dtype=np.float32)),
        "w_down": np.ascontiguousarray(np.asarray(inputs["L0_ffn_w_down"],
                                                  dtype=np.float32)),
        "invg": rep(inputs["L0_ffn_inv_gate"], D),
        "invu": rep(inputs["L0_ffn_inv_up"], D),
        "invi": rep(inputs["L0_ffn_inv_inter"], II),
        "swg": rep(inputs["L0_ffn_s_wgate"], II),
        "swu": rep(inputs["L0_ffn_s_wup"], II),
        "swd": rep(inputs["L0_ffn_s_wdown"], D),
        "idb": idb, "idf": idf,
    }
    in_maps = []
    for c in range(NCORES):
        m = dict(base)
        m["x"] = np.ascontiguousarray(xs[c * T:(c + 1) * T, :])
        in_maps.append(m)
    return in_maps


def kernel(**inputs):
    nc = build_nc()
    in_maps = _pack_inputs(inputs)
    res = run_bass_kernel_spmd(nc, in_maps, core_ids=list(range(NCORES)))
    out = np.concatenate([np.asarray(r["y"]) for r in res.results], axis=0)
    return out.reshape(1, 8192, D).astype(np.float32)
